# version 02
# baseline (speedup 1.0000x reference)
import numpy as np
import ml_dtypes
import concourse.bass as bass
import concourse.mybir as mybir
from concourse.bass_utils import run_bass_kernel_spmd

F32 = mybir.dt.float32
BF16 = mybir.dt.bfloat16
I32 = mybir.dt.int32
OP = mybir.AluOpType
AF = mybir.ActivationFunctionType
AX = mybir.AxisListType

D = 1024
T = 128
NPRE = 4096
NOWN = 4096
NT_LITE = NPRE // T
NT_OWN = NOWN // T
NT = NT_LITE + NT_OWN
GRP = 4
EPS = 1e-5

_STAGE = 0
_SERIAL = False

ENGS = ("sync", "gpsimd", "tensor", "vector", "scalar")


class Rec:

    def __init__(self):
        self.ops = {e: [] for e in ENGS}
        self.val = {}

    def emit(self, eng, fn):
        self.ops[eng].append(fn)

    def wait(self, eng, sem, v):
        if v > 0:
            self.emit(eng, lambda e, s=sem, vv=v: e.wait_ge(s, vv))

    def inc(self, sem, k=1):
        self.val[sem] = self.val.get(sem, 0) + k
        return self.val[sem]

    def cur(self, sem):
        return self.val.get(sem, 0)


def build_nc():
    nc = bass.Bass("TRN2", target_bir_lowering=False)

    dt_ = nc.dram_tensor
    x_all = dt_("x_all", [NPRE + NOWN, D], F32, kind="ExternalInput").ap()
    o_out = dt_("o", [NOWN, D], F32, kind="ExternalOutput").ap()
    h_wqkv = dt_("wqkv_t", [128, 3, 8, D], F32, kind="ExternalInput").ap()
    h_wcv = dt_("wcv_r", [128, 4, 4, 2, 256], F32, kind="ExternalInput").ap()
    h_wb = dt_("wb_t", [128, 4, 8, D], F32, kind="ExternalInput").ap()
    h_wu = dt_("wu_t", [128, 2, 16, D], F32, kind="ExternalInput").ap()
    h_fw = dt_("fw_r", [128, 8, D], F32, kind="ExternalInput").ap()
    h_gattn = dt_("gattn_r", [128, 8], F32, kind="ExternalInput").ap()
    h_battn = dt_("battn_r", [128, 8], F32, kind="ExternalInput").ap()
    h_gconv = dt_("gconv_r", [128, 8], F32, kind="ExternalInput").ap()
    h_bconv = dt_("bconv_r", [128, 8], F32, kind="ExternalInput").ap()
    h_gb_bc = dt_("gb_bc", [128, D], F32, kind="ExternalInput").ap()
    h_bb_bc = dt_("bb_bc", [128, D], F32, kind="ExternalInput").ap()
    h_bbr = dt_("bb_r", [128, 32], F32, kind="ExternalInput").ap()
    h_rows = {n: dt_(n, [1, D], F32, kind="ExternalInput").ap()
              for n in ("bq", "bk", "bv", "cb", "bu", "fb")}
    h_U = dt_("Utri", [128, 128], BF16, kind="ExternalInput").ap()
    h_id = dt_("ident", [128, 128], BF16, kind="ExternalInput").ap()
    h_ones = dt_("ones1", [128, 128], BF16, kind="ExternalInput").ap()
    h_stab = dt_("stab", [128, NT], F32, kind="ExternalInput").ap()
    h_mask = dt_("mask", [128, 1], F32, kind="ExternalInput").ap()
    h_imask = dt_("imask", [128, 1], F32, kind="ExternalInput").ap()
    h_c75 = dt_("c75", [128, 1], F32, kind="ExternalInput").ap()
    h_zmg = dt_("zmg", [128, 24], F32, kind="ExternalInput").ap()
    h_e3 = dt_("e3neg", [1, 384], BF16, kind="ExternalInput").ap()

    import contextlib
    ctx = contextlib.ExitStack()
    sb = lambda n, sh, dt=F32: ctx.enter_context(nc.sbuf_tensor(n, sh, dt))
    psf = lambda n, sh: ctx.enter_context(nc.psum_tensor(n, sh, F32))
    sem = lambda n: ctx.enter_context(nc.semaphore(n))

    s_wqkv = sb("s_wqkv", [128, 3, 8, D], BF16)
    s_wcv = sb("s_wcv", [128, 4, 4, 2, 256], BF16)
    s_wdT = sb("s_wdT", [128, 8, D], BF16)
    s_ar = sb("s_ar", [128, 16, D], BF16)
    s_gbbc = sb("s_gbbc", [128, D], F32)
    s_bbbc = sb("s_bbbc", [128, D], F32)
    s_gelu = sb("s_gelu", [128, 32, 512], BF16)
    s_hbT = sb("s_hbT", [128, 8, 512], BF16)
    s_x = sb("s_x", [128, D], F32)
    s_xn = sb("s_xn", [128, D], F32)
    s_a = sb("s_a", [128, D], F32)
    s_hst = sb("s_hst", [128, D], BF16)
    s_lt = sb("s_lt", [128, 8, 128], BF16)
    s_hcT = sb("s_hcT", [128, 8, 131], BF16)
    s_q = sb("s_q", [128, D], BF16)
    s_k = sb("s_k", [128, D], BF16)
    s_v = sb("s_v", [128, D], BF16)
    S1 = sb("S1", [128, D], F32)
    S2 = sb("S2", [128, D], F32)
    S3 = sb("S3", [128, D], F32)
    s_car = sb("s_car", [128, D], BF16)
    s_cq = s_car[0:1, :]
    s_cm = s_car[32:33, :]
    s_b3a = sb("s_b3a", [128, D], BF16)
    s_b3b = sb("s_b3b", [128, D], BF16)
    s_rows = {"bq": s_b3a[0:1, :], "bk": s_b3a[32:33, :], "bv": s_b3a[64:65, :],
              "cb": s_b3b[0:1, :], "bu": s_b3b[32:33, :], "fb": s_b3b[64:65, :]}
    s_bbr = sb("s_bbr", [128, 32], F32)
    s_gattn = sb("s_gattn", [128, 8], F32)
    s_battn = sb("s_battn", [128, 8], F32)
    s_battnb = sb("s_battnb", [128, 8], BF16)
    s_gconv = sb("s_gconv", [128, 8], F32)
    s_bconv = sb("s_bconv", [128, 8], F32)
    s_bconvb = sb("s_bconvb", [128, 8], BF16)
    s_U = sb("s_U", [128, 128], BF16)
    s_id = sb("s_id", [128, 128], BF16)
    s_ones = sb("s_ones", [128, 128], BF16)
    s_e3 = sb("s_e3", [1, 384], BF16)
    s_stab = sb("s_stab", [128, NT], F32)
    s_srec = sb("s_srec", [128, NT], F32)
    s_mask = sb("s_mask", [128, 1], F32)
    s_imask = sb("s_imask", [128, 1], F32)
    s_c75 = sb("s_c75", [128, 1], F32)
    s_rc75 = sb("s_rc75", [128, 1], F32)
    s_st = sb("s_st", [128, 8], F32)
    s_amax = sb("s_amax", [128, 32], F32)
    s_sc = sb("s_sc", [128, 32], F32)
    s_rsc = sb("s_rsc", [128, 32], F32)

    pf0 = s_gelu[0:1, 0:6, :].rearrange("p a b -> p (a b)")
    pf32 = s_hbT[0:1, 0:6, :].rearrange("p a b -> p (a b)")

    P_A = psf("P_A", [128, D])
    P_B = psf("P_B", [128, D])
    P_C = psf("P_C", [128, D])
    P_D = psf("P_D", [128, 512])
    P_T = ctx.enter_context(nc.psum_tensor("P_T", [128, D], BF16))

    sY = sem("sY")
    sG = sem("sG")
    sC = sem("sC")
    sA = sem("sA")
    sT = sem("sT")
    sV = sem("sV")
    sS = sem("sS")

    r = Rec()
    E = r.emit

    def y_dma(dst, src):
        v = r.inc(sY, 16)
        E("sync", lambda e, d=dst, s=src: e.dma_start(out=d, in_=s).then_inc(sY, 16))
        return v

    def g_dma(dst, src):
        v = r.inc(sG, 16)
        E("gpsimd", lambda e, d=dst, s=src: e.dma_start(out=d, in_=s).then_inc(sG, 16))
        return v

    def c_dma(dst, src):
        v = r.inc(sC, 16)
        E("sync", lambda e, d=dst, s=src: e.dma_start(out=d, in_=s).then_inc(sC, 16))
        return v

    def c_dma2(dst, src):
        v = r.inc(sC, 16)
        E("gpsimd", lambda e, d=dst, s=src: e.dma_start(out=d, in_=s).then_inc(sC, 16))
        return v

    def a_dma(dst, src):
        v = r.inc(sA, 16)
        E("gpsimd", lambda e, d=dst, s=src: e.dma_start(out=d, in_=s).then_inc(sA, 16))
        return v

    def t_mm(out, lhsT, rhs, start, stop, inc=False):
        if inc:
            v = r.inc(sT)
            E("tensor", lambda e, o=out, l=lhsT, rr=rhs, st=start, sp=stop:
              e.matmul(o, l, rr, start=st, stop=sp).then_inc(sT, 1))
            return v
        E("tensor", lambda e, o=out, l=lhsT, rr=rhs, st=start, sp=stop:
          e.matmul(o, l, rr, start=st, stop=sp))
        return None

    def t_tr(out, in_, inc=False):
        if inc:
            v = r.inc(sT)
            E("tensor", lambda e, o=out, i=in_: e.transpose(o, i, s_id[:, :]).then_inc(sT, 1))
            return v
        E("tensor", lambda e, o=out, i=in_: e.transpose(o, i, s_id[:, :]))
        return None

    def v_op(fn, inc=False):
        if inc:
            v = r.inc(sV)
            E("vector", lambda e, f=fn: f(e).then_inc(sV, 1))
            return v
        E("vector", fn)
        return None

    def s_op(fn, inc=False):
        if inc:
            v = r.inc(sS)
            E("scalar", lambda e, f=fn: f(e).then_inc(sS, 1))
            return v
        E("scalar", fn)
        return None

    def tr_batch():
        for dc in range(7):
            t_tr(P_T[:, dc * 128:(dc + 1) * 128], s_hst[:, dc * 128:(dc + 1) * 128])
        return t_tr(P_T[:, 7 * 128:8 * 128], s_hst[:, 7 * 128:8 * 128], inc=True)

    last_stats_S = [0]

    def ln_stats(src, wait_v=0, wait_y=0):
        if wait_y:
            r.wait("scalar", sY, wait_y)
        if wait_v:
            r.wait("scalar", sV, wait_v)
        s_op(lambda e, s=src: e.activation(S1[:, :], s, AF.Identity, accum_out=s_st[:, 0:1]))
        m = s_op(lambda e, s=src: e.activation(S1[:, :], s, AF.Square, accum_out=s_st[:, 1:2]), inc=True)
        last_stats_S[0] = m
        r.wait("vector", sS, m)
        v_op(lambda e: e.tensor_single_scalar(s_st[:, 2:3], s_st[:, 0:1], 1.0 / D, op=OP.mult))
        v_op(lambda e: e.tensor_single_scalar(s_st[:, 3:4], s_st[:, 1:2], 1.0 / D, op=OP.mult))
        v_op(lambda e: e.tensor_tensor(s_st[:, 4:5], s_st[:, 2:3], s_st[:, 2:3], op=OP.mult))
        v_op(lambda e: e.tensor_tensor(s_st[:, 3:4], s_st[:, 3:4], s_st[:, 4:5], op=OP.subtract))
        m = v_op(lambda e: e.tensor_single_scalar(s_st[:, 3:4], s_st[:, 3:4], EPS, op=OP.add), inc=True)
        r.wait("scalar", sV, m)
        m = s_op(lambda e: e.sqrt(s_st[:, 4:5], s_st[:, 3:4]), inc=True)
        r.wait("vector", sS, m)
        return v_op(lambda e: e.reciprocal(s_st[:, 5:6], s_st[:, 4:5]), inc=True)

    def ln_norm(out, src):
        return v_op(lambda e, o=out, s=src:
                    e.tensor_scalar(o, s, s_st[:, 2:3], s_st[:, 5:6], OP.subtract, OP.mult), inc=True)

    def quantize(dst_bf, src_f32):
        src3 = src_f32.rearrange("p (b w) -> p b w", w=32)
        v_op(lambda e, s3=src3: e.tensor_reduce(s_amax[:, :], s3, axis=AX.X, op=OP.max,
                                                apply_absolute_value=True))
        am3 = s_amax[:, :].rearrange("p (b o) -> p b o", o=1).broadcast_to([128, 32, 32])
        S13 = S1[:, :].rearrange("p (b w) -> p b w", w=32)
        v_op(lambda e, a=am3, o=S13: e.tensor_copy(o, a))
        v_op(lambda e: e.tensor_single_scalar(S1[:, :], S1[:, :], 1.0 / 7.5, op=OP.mult))
        v_op(lambda e: e.tensor_single_scalar(S1[:, :], S1[:, :], 1e-30, op=OP.max))
        v_op(lambda e: e.reciprocal(S2[:, :], S1[:, :]))
        v_op(lambda e: e.tensor_tensor(S3[:, :], src_f32, S2[:, :], op=OP.mult))
        v_op(lambda e: e.tensor_single_scalar(S2[:, :], S3[:, :], -1.0, op=OP.mult))
        v_op(lambda e: e.tensor_tensor(S2[:, :], S2[:, :], S3[:, :], op=OP.max))
        v_op(lambda e: e.tensor_single_scalar(s_q[:, :], S2[:, :], 4.0, op=OP.is_ge))
        v_op(lambda e: e.tensor_single_scalar(S2[:, :], S2[:, :], 2.0, op=OP.is_ge))
        v_op(lambda e: e.tensor_scalar(s_v[:, :], S2[:, :], 0.125, 0.125, OP.mult, OP.add))
        v_op(lambda e: e.scalar_tensor_tensor(s_v[:, :], s_q[:, :], 0.25, s_v[:, :],
                                              op0=OP.mult, op1=OP.add))
        v_op(lambda e: e.tensor_scalar(S2[:, :], S2[:, :], -4.0, 8.0, OP.mult, OP.add))
        v_op(lambda e: e.scalar_tensor_tensor(S2[:, :], s_q[:, :], -2.0, S2[:, :],
                                              op0=OP.mult, op1=OP.add))
        v_op(lambda e: e.tensor_tensor(S3[:, :], S3[:, :], S2[:, :], op=OP.mult))
        v_op(lambda e: e.tensor_single_scalar(S3[:, :], S3[:, :], 0.5, op=OP.subtract))
        v_op(lambda e: e.tensor_copy(S2[:, :].bitcast(I32), S3[:, :]))
        v_op(lambda e: e.tensor_copy(s_k[:, :], S2[:, :].bitcast(I32)))
        v_op(lambda e: e.tensor_tensor(S2[:, :], s_k[:, :], S3[:, :], op=OP.is_lt))
        v_op(lambda e: e.tensor_tensor(S2[:, :], s_k[:, :], S2[:, :], op=OP.add))
        v_op(lambda e: e.tensor_tensor(S2[:, :], S2[:, :], s_v[:, :], op=OP.mult))
        v_op(lambda e: e.tensor_tensor(S3[:, :], S2[:, :], S1[:, :], op=OP.mult))
        return v_op(lambda e, o=dst_bf: e.tensor_copy(o, S3[:, :]), inc=True)

    for dst, src in ((s_U, h_U), (s_id, h_id), (s_ones, h_ones), (s_e3, h_e3),
                     (s_stab, h_stab), (s_mask, h_mask), (s_imask, h_imask), (s_c75, h_c75),
                     (s_gattn, h_gattn), (s_battn, h_battn), (s_gconv, h_gconv),
                     (s_bconv, h_bconv), (s_gbbc, h_gb_bc), (s_bbbc, h_bb_bc),
                     (s_bbr, h_bbr)):
        y_dma(dst[...], src[...])
    for n in ("bq", "bk", "bv", "cb", "bu", "fb"):
        g_bias_last = g_dma(s_rows[n], h_rows[n][:, :])
    g_zmg = g_dma(s_hcT[:, :, 0:3], h_zmg[:, :].rearrange("p (a b) -> p a b", b=3))
    g_wq = g_dma(s_wqkv[...], h_wqkv[...])
    g_wc = g_dma(s_wcv[...], h_wcv[...])
    consts_y = r.cur(sY)

    r.wait("vector", sY, consts_y)
    v_op(lambda e: e.reciprocal(s_srec[:, :], s_stab[:, :]))
    rc_m = v_op(lambda e: e.reciprocal(s_rc75[:, :], s_c75[:, :]), inc=True)
    r.wait("vector", sG, r.cur(sG))
    bat = v_op(lambda e: e.tensor_copy(s_battnb[:, :], s_battn[:, :]), inc=True)
    bcv = v_op(lambda e: e.tensor_copy(s_bconvb[:, :], s_bconv[:, :]), inc=True)

    r.wait("tensor", sV, bcv)
    bias_base = {"bq": 0, "bk": 32, "bv": 64, "cb": 0, "bu": 32, "fb": 64}
    for p, name in enumerate(("bq", "bk", "bv")):
        dst = s_rows[name]
        bp = bias_base[name]
        for half in range(2):
            for k in range(8):
                t_mm(P_D[0:1, :], s_battnb[:, k:k + 1], s_wqkv[:, p, k, half * 512:(half + 1) * 512],
                     start=(k == 0), stop=False)
            mdone = t_mm(P_D[0:1, :], s_ones[bp:bp + 1, 0:1], dst[0:1, half * 512:(half + 1) * 512],
                         start=False, stop=True, inc=True)
            r.wait("vector", sT, mdone)
            ev = v_op(lambda e, h=half: e.tensor_copy(s_car[0:1, h * 512:(h + 1) * 512], P_D[0:1, :]), inc=True)
            r.wait("tensor", sV, ev)
        if bp == 0:
            cp = v_op(lambda e, d=dst: e.tensor_copy(d, s_car[0:1, :]), inc=True)
            r.wait("tensor", sV, cp)
        else:
            r.wait("sync", sV, ev)
            yv = y_dma(dst, s_car[0:1, :])
            r.wait("vector", sY, yv)
            r.wait("tensor", sY, yv)
    for g in range(4):
        for tau in range(4):
            for kc in range(2):
                t_mm(P_D[0:1, 0:256], s_bconvb[:, g * 2 + kc:g * 2 + kc + 1], s_wcv[:, g, tau, kc, :],
                     start=(tau == 0 and kc == 0), stop=False)
        mdone = t_mm(P_D[0:1, 0:256], s_ones[0:1, 0:1], s_rows["cb"][0:1, g * 256:(g + 1) * 256],
                     start=False, stop=True, inc=True)
        r.wait("vector", sT, mdone)
        ev = v_op(lambda e, gg=g: e.tensor_copy(s_rows["cb"][0:1, gg * 256:(gg + 1) * 256], P_D[0:1, 0:256]), inc=True)
        r.wait("tensor", sV, ev)

    for s in range(3):
        for hh in range(2):
            pfd = None
            for g in (0, 1) if hh == 0 else (2, 3):
                first_mm = True
                for kc in range(2):
                    for tau in range(0, 3 - s):
                        pfd = t_mm(P_D[0:1, (g % 2) * 256:(g % 2) * 256 + 256],
                                   s_bconvb[:, g * 2 + kc:g * 2 + kc + 1], s_wcv[:, g, tau, kc, :],
                                   start=first_mm, stop=(kc == 1 and tau == 2 - s),
                                   inc=(g % 2 == 1 and kc == 1 and tau == 2 - s))
                        first_mm = False
            r.wait("vector", sT, pfd)
            ev = v_op(lambda e, ss=s, h2=hh: e.tensor_copy(
                pf0[0:1, ss * 1024 + h2 * 512:ss * 1024 + (h2 + 1) * 512], P_D[0:1, :]), inc=True)
            r.wait("tensor", sV, ev)
    pf_m = v_op(lambda e: e.tensor_scalar(pf32[0:1, :], pf0[0:1, :], s_imask[0:1, 0:1], None, OP.mult), inc=True)

    r.wait("vector", sT, r.cur(sT))
    for p in range(3):
        for k in range(8):
            v_op(lambda e, pp=p, kk=k: e.tensor_scalar(
                s_wqkv[:, pp, kk, :], s_wqkv[:, pp, kk, :], s_gattn[:, kk:kk + 1], None, OP.mult))
    for g in range(4):
        for kc in range(2):
            v_op(lambda e, gg=g, kk=kc: e.tensor_scalar(
                s_wcv[:, gg, :, kk, :], s_wcv[:, gg, :, kk, :], s_gconv[:, gg * 2 + kk:gg * 2 + kk + 1],
                None, OP.mult))

    r.wait("vector", sV, rc_m)
    for oc in range(8):
        ly = y_dma(s_xn[:, :], h_fw[:, oc, :])
        r.wait("vector", sY, ly)
        qd = quantize(s_hst[:, :], s_xn[:, :])
        r.wait("tensor", sV, qd)
        td = tr_batch()
        r.wait("vector", sT, td)
        ev = v_op(lambda e, o=oc: e.tensor_copy(
            s_wdT[:, :, o * 128:(o + 1) * 128], P_T[:, :].rearrange("p (a b) -> p a b", b=128)), inc=True)
        r.wait("sync", sV, ev)

    if _STAGE == 9:
        E("gpsimd", lambda e: e.wait_ge(sV, r.cur(sV)))
        for k in range(8):
            g_dma(o_out[k * 128:(k + 1) * 128, :], s_wdT[:, k, :])
        g_dma(o_out[1024:1152, :], s_hst[:, :])
        g_dma(o_out[1152:1280, :], s_xn[:, :])
        E("gpsimd", lambda e: e.wait_ge(sG, r.cur(sG)))
        with ctx:
            with nc.Block() as block:
                for eng in ENGS:
                    def runner(e, ops=tuple(r.ops[eng])):
                        for f in ops:
                            f(e)
                    getattr(block, eng)(runner)
        return nc

    if _STAGE == 7:
        r.wait("gpsimd", sV, pf_m)
        g_dma(o_out[0:3, :].rearrange("(a p) b -> p (a b)", p=1), pf0[0:1, :])
        g_dma(o_out[4:7, :].rearrange("(a p) b -> p (a b)", p=1), pf32[0:1, :])
        E("gpsimd", lambda e: e.wait_ge(sG, r.cur(sG)))
        with ctx:
            with nc.Block() as block:
                for eng in ENGS:
                    def runner(e, ops=tuple(r.ops[eng])):
                        for f in ops:
                            f(e)
                    getattr(block, eng)(runner)
        return nc

    startup_V = r.cur(sV)

    last_pout_store = 0
    carry_dma_q = 0
    carry_dma_m = 0
    m_PA_free = startup_V
    m_PB_free = startup_V
    m_PC_free = startup_V
    m_PT_free = r.cur(sV)
    m_hcT_used = r.cur(sT)
    m_hst_T = r.cur(sT)
    m_x_freeV = r.cur(sV)
    m_x_statS = 0
    m_xn_statS = 0
    m_a_statS = 0
    m_quant_prev = startup_V
    m_boomstoreY = 0
    grp_pout_y = []
    grp_hbT = []
    arena_free_T = r.cur(sT)

    for til in range(NT):
        if _SERIAL:
            for _eng in ENGS:
                for _sem in (sY, sG, sT, sV, sS):
                    r.wait(_eng, _sem, r.cur(_sem))
        own = til >= NT_LITE
        tj = til - NT_LITE
        first = til == 0
        at_boundary = til == NT_LITE

        r.wait("sync", sY, last_pout_store)
        r.wait("sync", sV, m_x_freeV)
        r.wait("sync", sS, m_x_statS)
        lx = y_dma(s_x[:, :], x_all[til * T:(til + 1) * T, :])

        ln_stats(s_x[:, :], wait_v=m_quant_prev, wait_y=max(lx, m_boomstoreY))
        m_x_statS = last_stats_S[0]
        r.wait("vector", sS, m_xn_statS)
        xn_m = ln_norm(s_xn[:, :], s_x[:, :])
        m_x_freeV = xn_m

        if own and _STAGE == 4:
            r.wait("sync", sV, xn_m)
            last_pout_store = y_dma(o_out[tj * T:(tj + 1) * T, :], s_xn[:, :])
            m_quant_prev = xn_m
            m_x_freeV = xn_m
            continue
        ln_stats(s_xn[:, :], wait_v=xn_m)
        m_xn_statS = last_stats_S[0]
        r.wait("vector", sT, m_hst_T)
        hc_m = ln_norm(s_hst[:, :], s_xn[:, :])

        if own and _STAGE == 5:
            r.wait("vector", sV, hc_m)
            s5 = v_op(lambda e: e.tensor_copy(S3[:, :], s_hst[:, :]), inc=True)
            r.wait("sync", sV, s5)
            last_pout_store = y_dma(o_out[tj * T:(tj + 1) * T, :], S3[:, :])
            m_quant_prev = s5
            m_x_freeV = s5
            continue
        if not first:
            r.wait("vector", sT, m_hcT_used)
            v_op(lambda e: e.tensor_copy(s_hcT[:, :, 0:3], s_hcT[:, :, 128:131]))
        else:
            r.wait("tensor", sG, r.cur(sG))
        r.wait("tensor", sV, hc_m)
        r.wait("tensor", sV, m_PT_free)
        tr_m = tr_batch()
        m_hst_T = tr_m
        r.wait("vector", sT, tr_m)
        ev_m = v_op(lambda e: e.tensor_copy(s_hcT[:, :, 3:131],
                                            P_T[:, :].rearrange("p (a b) -> p a b", b=128)), inc=True)
        m_PT_free = ev_m

        r.wait("tensor", sV, ev_m)
        r.wait("tensor", sV, m_PA_free)
        for g in range(4):
            for tau in range(4):
                for kc in range(2):
                    t_mm(P_A[:, g * 256:(g + 1) * 256], s_hcT[:, g * 2 + kc, tau:tau + 128],
                         s_wcv[:, g, tau, kc, :],
                         start=(g in (0, 2) and tau == 0 and kc == 0), stop=False)
        if til == 0 or til == NT_LITE:
            pfx = pf0 if til == 0 else pf32
            for s in range(3):
                t_mm(P_A[:, 0:512], s_e3[0:1, s * 128:(s + 1) * 128],
                     pfx[0:1, s * 1024:s * 1024 + 512], start=False, stop=False)
                t_mm(P_A[:, 512:1024], s_e3[0:1, s * 128:(s + 1) * 128],
                     pfx[0:1, s * 1024 + 512:(s + 1) * 1024], start=False, stop=False)
        t_mm(P_A[:, 0:512], s_ones[0:1, :], s_rows["cb"][0:1, 0:512], start=False, stop=True)
        conv_m = t_mm(P_A[:, 512:1024], s_ones[0:1, :], s_rows["cb"][0:1, 512:1024],
                      start=False, stop=True, inc=True)
        m_hcT_used = conv_m

        r.wait("vector", sT, conv_m)
        if _STAGE in (1, 2):
            r.wait("vector", sY, last_pout_store)
        r.wait("vector", sS, m_a_statS)
        a1_m = v_op(lambda e: e.tensor_tensor(s_a[:, :], s_xn[:, :], P_A[:, :], op=OP.add), inc=True)

        if own and _STAGE == 6:
            r.wait("sync", sV, a1_m)
            last_pout_store = y_dma(o_out[tj * T:(tj + 1) * T, :], s_a[:, :])
            m_quant_prev = a1_m
            m_x_freeV = a1_m
            m_PA_free = a1_m
            continue
        ln_stats(s_a[:, :], wait_v=a1_m)
        m_a_statS = last_stats_S[0]
        r.wait("vector", sT, m_hst_T)
        ha_m = ln_norm(s_hst[:, :], s_a[:, :])

        r.wait("tensor", sV, ha_m)
        r.wait("tensor", sV, m_PT_free)
        tr_m = tr_batch()
        m_hst_T = tr_m
        r.wait("vector", sT, tr_m)
        ev_m = v_op(lambda e: e.tensor_copy(s_lt[:, :, :],
                                            P_T[:, :].rearrange("p (a b) -> p a b", b=128)), inc=True)
        m_PT_free = ev_m
        haT_m = ev_m

        r.wait("tensor", sV, haT_m)
        projs = [(0, s_q, "bq"), (1, s_k, "bk")] + ([(2, s_v, "bv")] if own else [])
        pe_free = m_PB_free
        for p, dstb, bn in projs:
            bias = s_rows[bn]
            bp = {"bq": 0, "bk": 32, "bv": 64}[bn]
            r.wait("tensor", sV, pe_free)
            pj_m = None
            for half in range(2):
                for k in range(8):
                    t_mm(P_B[:, half * 512:(half + 1) * 512], s_lt[:, k, :],
                         s_wqkv[:, p, k, half * 512:(half + 1) * 512],
                         start=(k == 0), stop=False)
                pj_m = t_mm(P_B[:, half * 512:(half + 1) * 512], s_ones[bp:bp + 1, :],
                            bias[0:1, half * 512:(half + 1) * 512], start=False, stop=True,
                            inc=(half == 1))
            r.wait("vector", sT, pj_m)
            pe_free = v_op(lambda e, d=dstb: e.tensor_copy(d[:, :], P_B[:, :]), inc=True)
        qkv_evac = pe_free
        m_PB_free = qkv_evac

        r.wait("tensor", sV, qkv_evac)
        r.wait("tensor", sV, a1_m)
        r.wait("tensor", sC, max(carry_dma_q, carry_dma_m))
        csq_m = None
        for half in range(2):
            csq_m = t_mm(P_A[:, half * 512:(half + 1) * 512], s_U[:, :],
                         s_q[:, half * 512:(half + 1) * 512],
                         start=True, stop=first, inc=(first and half == 1))
            if not first:
                csq_m = t_mm(P_A[:, half * 512:(half + 1) * 512], s_ones[0:1, :],
                             s_cq[0:1, half * 512:(half + 1) * 512], start=False, stop=True,
                             inc=(half == 1))

        r.wait("vector", sT, csq_m)
        r.wait("vector", sC, carry_dma_q)
        stq = v_op(lambda e: e.tensor_copy(s_b3a[96:128, :], P_A[96:128, :]), inc=True)
        r.wait("sync", sV, stq)
        carry_dma_q = c_dma(s_cq[0:1, :], s_b3a[127:128, :])
        m_m = v_op(lambda e, tt=til: e.scalar_tensor_tensor(
            s_q[:, :], P_A[:, :], s_srec[:, tt:tt + 1], s_k[:, :], op0=OP.mult, op1=OP.mult), inc=True)
        m_PA_free = m_m

        r.wait("tensor", sV, m_m)
        r.wait("tensor", sV, m_PC_free)
        csm_m = None
        for half in range(2):
            csm_m = t_mm(P_C[:, half * 512:(half + 1) * 512], s_U[:, :],
                         s_q[:, half * 512:(half + 1) * 512],
                         start=True, stop=first, inc=(first and half == 1))
            if not first:
                csm_m = t_mm(P_C[:, half * 512:(half + 1) * 512], s_ones[32:33, :],
                             s_cm[0:1, half * 512:(half + 1) * 512], start=False, stop=True,
                             inc=(half == 1))

        r.wait("vector", sT, csm_m)
        r.wait("vector", sC, carry_dma_m)
        cm_row = v_op(lambda e: e.tensor_copy(s_b3b[96:128, :], P_C[96:128, :]), inc=True)
        r.wait("sync", sV, cm_row)
        carry_dma_m = c_dma(s_cm[0:1, :], s_b3b[127:128, :])

        if own and _STAGE != 1:
            t1_m = v_op(lambda e, tt=til: e.scalar_tensor_tensor(
                S2[:, :], P_C[:, :], s_srec[:, tt:tt + 1], s_v[:, :], op0=OP.mult, op1=OP.mult), inc=True)
            m_PC_free = t1_m
            a2_m = v_op(lambda e: e.tensor_tensor(s_a[:, :], s_a[:, :], S2[:, :], op=OP.add), inc=True)
            m_quant_prev = a2_m
        else:
            m_PC_free = cm_row
            a2_m = cm_row
            m_quant_prev = cm_row

        if til == NT_LITE - 1:
            r.wait("vector", sC, max(carry_dma_q, carry_dma_m))
            v_op(lambda e: e.tensor_scalar(s_cq[0:1, :], s_cq[0:1, :], s_mask[0:1, 0:1], None, OP.mult))
            mk = v_op(lambda e: e.tensor_scalar(s_cm[0:1, :], s_cm[0:1, :], s_mask[32:33, 0:1], None, OP.mult), inc=True)
            r.wait("tensor", sV, mk)

        if not own:
            continue

        if _STAGE in (1, 2):
            r.wait("sync", sV, a2_m)
            last_pout_store = y_dma(o_out[tj * T:(tj + 1) * T, :], s_a[:, :])
            m_quant_prev = a2_m
            continue

        ln_stats(s_a[:, :], wait_v=a2_m)
        m_a_statS = last_stats_S[0]
        ln_norm(S2[:, :], s_a[:, :])
        v_op(lambda e: e.tensor_tensor(S3[:, :], S2[:, :], s_gbbc[:, :], op=OP.mult))
        r.wait("vector", sT, m_hst_T)
        hb_m = v_op(lambda e: e.tensor_tensor(s_hst[:, :], S3[:, :], s_bbbc[:, :], op=OP.add), inc=True)

        r.wait("tensor", sV, hb_m)
        r.wait("tensor", sV, m_PT_free)
        tr_m = tr_batch()
        m_hst_T = tr_m
        r.wait("vector", sT, tr_m)
        toff = (tj % GRP) * 128
        ev_m = v_op(lambda e, to=toff: e.tensor_copy(
            s_hbT[:, :, to:to + 128], P_T[:, :].rearrange("p (a b) -> p a b", b=128)), inc=True)
        m_PT_free = ev_m
        grp_hbT.append(ev_m)

        r.wait("vector", sY, m_boomstoreY)
        r.wait("vector", sT, m_hst_T)
        qd_m = quantize(s_hst[:, :], s_xn[:, :])
        m_quant_prev = qd_m

        if _STAGE == 8:
            r.wait("gpsimd", sV, qd_m)
            last_pout_store = r.cur(sY)
            if tj == 0:
                g_dma(o_out[0:T, 0:512], s_hst[:, 0:512])
                g_dma(o_out[0:T, 512:544], s_amax[:, :])
                g_dma(o_out[0:T, 544:576], s_sc[:, :])
                gd = g_dma(o_out[0:T, 576:608], s_rsc[:, :])
                r.wait("vector", sG, gd)
            m_x_freeV = qd_m
            r.wait("vector", sG, r.cur(sG))
            continue
        r.wait("tensor", sV, qd_m)
        r.wait("tensor", sV, m_PT_free)
        tr_m = tr_batch()
        m_hst_T = tr_m
        r.wait("vector", sT, tr_m)
        ev_m = v_op(lambda e: e.tensor_copy(s_lt[:, :, :],
                                            P_T[:, :].rearrange("p (a b) -> p a b", b=128)), inc=True)
        m_PT_free = ev_m

        r.wait("tensor", sV, ev_m)
        r.wait("tensor", sV, m_PB_free)
        F_m = None
        for half in range(2):
            for k in range(8):
                t_mm(P_B[:, half * 512:(half + 1) * 512], s_lt[:, k, :],
                     s_wdT[:, k, half * 512:(half + 1) * 512], start=(k == 0), stop=False)
            F_m = t_mm(P_B[:, half * 512:(half + 1) * 512], s_ones[64:65, :],
                       s_rows["fb"][0:1, half * 512:(half + 1) * 512], start=False, stop=True,
                       inc=(half == 1))

        r.wait("vector", sT, F_m)
        v_op(lambda e: e.tensor_tensor(s_x[:, :], s_x[:, :], P_B[:, :], op=OP.add))
        F_ev = v_op(lambda e: e.tensor_tensor(s_x[:, :], s_x[:, :], s_a[:, :], op=OP.add), inc=True)
        m_PB_free = F_ev
        m_x_freeV = F_ev
        r.wait("sync", sV, F_ev)
        last_pout_store = y_dma(o_out[tj * T:(tj + 1) * T, :], s_x[:, :])
        grp_pout_y.append(last_pout_store)

        if _STAGE == 3 or tj % GRP != GRP - 1:
            if tj % GRP == GRP - 1:
                grp_pout_y = []
                grp_hbT = []
            continue

        g0 = tj - (GRP - 1)
        hbT_ready = max(grp_hbT)
        b1_last = {}
        gelu_evacs = []
        q_loads = {}
        for qt in range(2):
            r.wait("gpsimd", sT, arena_free_T)
            q_loads[qt] = a_dma(s_ar[:, (qt % 2) * 8:(qt % 2 + 1) * 8, :], h_wb[:, qt, :, :])
        for qt in range(4):
            half = qt % 2
            r.wait("tensor", sA, q_loads[qt])
            if qt == 0:
                r.wait("tensor", sV, hbT_ready)
                r.wait("tensor", sV, m_PC_free)
            for ecl in range(8):
                ec = qt * 8 + ecl
                bank = ec % 2
                if len(gelu_evacs) >= 2:
                    r.wait("tensor", sS, gelu_evacs[-2])
                md = None
                for dc in range(8):
                    md = t_mm(P_C[:, bank * 512:(bank + 1) * 512],
                              s_ar[:, half * 8 + dc, ecl * 128:(ecl + 1) * 128],
                              s_hbT[:, dc, :], start=(dc == 0), stop=(dc == 7),
                              inc=(dc == 7))
                b1_last[qt] = md
                r.wait("scalar", sT, md)
                gv = s_op(lambda e, ee=ec, bk=bank: e.activation(
                    s_gelu[:, ee, :], P_C[:, bk * 512:(bk + 1) * 512], AF.Gelu,
                    bias=s_bbr[:, ee:ee + 1]), inc=True)
                gelu_evacs.append(gv)
            if qt < 2:
                r.wait("gpsimd", sT, b1_last[qt])
                q_loads[qt + 2] = a_dma(s_ar[:, half * 8:(half + 1) * 8, :], h_wb[:, qt + 2, :, :])
        r.wait("gpsimd", sT, b1_last[3])
        wu_l0 = a_dma(s_ar[:, :, :], h_wu[:, 0, :, :])
        r.wait("tensor", sV, m_PA_free)
        r.wait("tensor", sV, m_PB_free)
        r.wait("tensor", sS, gelu_evacs[-1])
        b2_done = []
        hf_last = 0
        for hf in range(2):
            if hf == 0:
                r.wait("tensor", sA, wu_l0)
            else:
                r.wait("gpsimd", sT, hf_last)
                wu_l1 = a_dma(s_ar[:, :, :], h_wu[:, 1, :, :])
                r.wait("tensor", sA, wu_l1)
            for tt in range(GRP):
                md = None
                for ecl in range(16):
                    ec = hf * 16 + ecl
                    for dh in range(2):
                        if tt < 3:
                            dst = (P_A, P_B, P_C)[tt][:, dh * 512:(dh + 1) * 512]
                        else:
                            dst = P_D[:, :] if dh == 0 else P_T[:, :].bitcast(F32)
                        mret = t_mm(dst, s_gelu[:, ec, tt * 128:(tt + 1) * 128],
                                    s_ar[:, ecl, dh * 512:(dh + 1) * 512],
                                    start=(hf == 0 and ecl == 0), stop=False,
                                    inc=(hf == 0 and tt == 3 and ecl == 15 and dh == 1))
                        if mret is not None:
                            hf_last = mret
                if hf == 1:
                    for dh in range(2):
                        if tt < 3:
                            dst = (P_A, P_B, P_C)[tt][:, dh * 512:(dh + 1) * 512]
                        else:
                            dst = P_D[:, :] if dh == 0 else P_T[:, :].bitcast(F32)
                        md = t_mm(dst, s_ones[32:33, :], s_rows["bu"][0:1, dh * 512:(dh + 1) * 512],
                                  start=False, stop=True, inc=(dh == 1))
                    b2_done.append(md)
        arena_free_T = b2_done[-1]
        r.wait("sync", sY, grp_pout_y[-1])
        r.wait("sync", sV, m_quant_prev)
        final_stores = []
        for tt in range(GRP):
            tloc = g0 + tt
            dstS = S1 if tt % 2 == 0 else S3
            lb = y_dma(dstS[:, :], o_out[tloc * T:(tloc + 1) * T, :])
            r.wait("vector", sY, lb)
            r.wait("vector", sT, b2_done[tt])
            if tt < 3:
                pt = (P_A, P_B, P_C)[tt]
                fv = v_op(lambda e, sbb=dstS, pp=pt: e.tensor_tensor(
                    sbb[:, :], sbb[:, :], pp[:, :], op=OP.add), inc=True)
            else:
                v_op(lambda e, sbb=dstS: e.tensor_tensor(
                    sbb[:, 0:512], sbb[:, 0:512], P_D[:, :], op=OP.add))
                fv = v_op(lambda e, sbb=dstS: e.tensor_tensor(
                    sbb[:, 512:1024], sbb[:, 512:1024], P_T[:, :].bitcast(F32), op=OP.add), inc=True)
            r.wait("sync", sV, fv)
            final_stores.append(y_dma(o_out[tloc * T:(tloc + 1) * T, :], dstS[:, :]))
        last_pout_store = final_stores[-1]
        m_boomstoreY = final_stores[-1]
        m_PA_free = r.cur(sV)
        m_PB_free = r.cur(sV)
        m_PC_free = r.cur(sV)
        grp_pout_y = []
        grp_hbT = []

    r.wait("sync", sY, r.cur(sY))

    with ctx:
        with nc.Block() as block:
            for eng in ENGS:
                def runner(e, ops=tuple(r.ops[eng])):
                    for f in ops:
                        f(e)
                getattr(block, eng)(runner)
    return nc


_NC_CACHE = {}


def _get_nc(stage):
    stage = (stage, _SERIAL)
    if stage not in _NC_CACHE:
        global _STAGE
        _STAGE = stage[0]
        _NC_CACHE[stage] = build_nc()
    return _NC_CACHE[stage]


def _prep_shared(inputs):
    f32 = lambda a: np.ascontiguousarray(np.asarray(a, dtype=np.float32))
    Wq = f32(inputs["L0_attn_Wq"]); Wk = f32(inputs["L0_attn_Wk"]); Wv = f32(inputs["L0_attn_Wv"])
    wqkv = np.stack([W.T.reshape(8, 128, D).transpose(1, 0, 2) for W in (Wq, Wk, Wv)], axis=1)
    cw = f32(inputs["L0_attn_conv_w"]).reshape(4, 256, 2, 128, 4)
    wcv = np.ascontiguousarray(cw.transpose(3, 0, 4, 2, 1))
    Wb = f32(inputs["L0_attn_Wb"])
    wb_t = np.ascontiguousarray(Wb.T.reshape(8, 128, 4, 1024).transpose(1, 2, 0, 3))
    Wu = f32(inputs["L0_attn_Wu"])
    wu_t = np.ascontiguousarray(Wu.T.reshape(2, 16, 128, 1024).transpose(2, 0, 1, 3))
    fw = f32(inputs["L0_ffn_weight"]).reshape(8, 128, D).transpose(1, 0, 2)
    rvec = lambda v: np.ascontiguousarray(f32(v).reshape(8, 128).T)
    bfc = lambda a: np.ascontiguousarray(a.astype(ml_dtypes.bfloat16))
    return {
        "wqkv_t": np.ascontiguousarray(wqkv),
        "wcv_r": wcv,
        "wb_t": wb_t,
        "wu_t": wu_t,
        "fw_r": np.ascontiguousarray(fw),
        "gattn_r": rvec(inputs["L0_attn_ln_attn_g"]),
        "battn_r": rvec(inputs["L0_attn_ln_attn_b"]),
        "gconv_r": rvec(inputs["L0_attn_ln_conv_g"]),
        "bconv_r": rvec(inputs["L0_attn_ln_conv_b"]),
        "gb_bc": np.ascontiguousarray(np.broadcast_to(f32(inputs["L0_attn_ln_boom_g"]), (128, D))),
        "bb_bc": np.ascontiguousarray(np.broadcast_to(f32(inputs["L0_attn_ln_boom_b"]), (128, D))),
        "bb_r": np.ascontiguousarray(f32(inputs["L0_attn_bb"]).reshape(32, 128).T),
        "bq": f32(inputs["L0_attn_bq"]).reshape(1, D),
        "bk": f32(inputs["L0_attn_bk"]).reshape(1, D),
        "bv": f32(inputs["L0_attn_bv"]).reshape(1, D),
        "cb": f32(inputs["L0_attn_conv_b"]).reshape(1, D),
        "bu": f32(inputs["L0_attn_bu"]).reshape(1, D),
        "fb": f32(inputs["L0_ffn_bias"]).reshape(1, D),
        "Utri": bfc(np.triu(np.ones((128, 128), np.float32))),
        "ident": bfc(np.eye(128, dtype=np.float32)),
        "ones1": bfc(np.ones((128, 128), np.float32)),
        "e3neg": bfc(np.concatenate([-np.eye(3, 128, k=s_, dtype=np.float32)[s_:s_+1] if False else -np.eye(128, dtype=np.float32)[s_:s_+1] for s_ in range(3)], axis=1)),
        "c75": np.full((128, 1), 7.5, np.float32),

    }


def kernel(**inputs):
    x = np.asarray(inputs["L0_attn_hidden_states"], dtype=np.float32)
    B, S, _ = x.shape
    shared = _prep_shared(inputs)
    p = np.arange(128, dtype=np.float32)[:, None]
    in_maps = []
    for c in range(8):
        b, h = c // 2, c % 2
        xpre = np.zeros((NPRE, D), np.float32) if h == 0 else np.ascontiguousarray(x[b, 0:NPRE])
        xown = np.ascontiguousarray(x[b, h * NOWN:(h + 1) * NOWN])
        jl = np.arange(NT_LITE, dtype=np.float32)[None, :]
        jo = np.arange(NT_OWN, dtype=np.float32)[None, :]
        stab = np.concatenate([jl * T + p + 1.0, h * NOWN + jo * T + p + 1.0], axis=1)
        im = dict(shared)
        im["x_all"] = np.concatenate([xpre, xown], axis=0)
        im["stab"] = np.ascontiguousarray(stab.astype(np.float32))
        im["mask"] = np.full((128, 1), float(h), np.float32)
        im["imask"] = np.full((128, 1), 1.0 - float(h), np.float32)
        im["zmg"] = np.zeros((128, 24), np.float32)
        in_maps.append(im)
    nc = _get_nc(_STAGE)
    res = run_bass_kernel_spmd(nc, in_maps, core_ids=list(range(8)))
    out = np.empty((B, S, D), np.float32)
    for c in range(8):
        b, h = c // 2, c % 2
        out[b, h * NOWN:(h + 1) * NOWN] = res.results[c]["o"]
    return out


# version 04
# speedup vs baseline: 1.0841x; 1.0841x over previous
import numpy as np
import ml_dtypes
import concourse.bass as bass
import concourse.mybir as mybir
from concourse.bass_utils import run_bass_kernel_spmd

F32 = mybir.dt.float32
BF16 = mybir.dt.bfloat16
I32 = mybir.dt.int32
OP = mybir.AluOpType
AF = mybir.ActivationFunctionType
AX = mybir.AxisListType

D = 1024
T = 128
NPRE = 4096
NOWN = 4096
NT_LITE = NPRE // T
NT_OWN = NOWN // T
NT = NT_LITE + NT_OWN
GRP = 4
EPS = 1e-5

_STAGE = 0
_SERIAL = False

ENGS = ("sync", "gpsimd", "tensor", "vector", "scalar")


class Rec:

    def __init__(self):
        self.ops = {e: [] for e in ENGS}
        self.val = {}

    def emit(self, eng, fn):
        self.ops[eng].append(fn)

    def wait(self, eng, sem, v):
        if v > 0:
            self.emit(eng, lambda e, s=sem, vv=v: e.wait_ge(s, vv))

    def inc(self, sem, k=1):
        self.val[sem] = self.val.get(sem, 0) + k
        return self.val[sem]

    def cur(self, sem):
        return self.val.get(sem, 0)


def build_nc():
    nc = bass.Bass("TRN2", target_bir_lowering=False)

    dt_ = nc.dram_tensor
    x_all = dt_("x_all", [NPRE + NOWN, D], F32, kind="ExternalInput").ap()
    o_out = dt_("o", [NOWN, D], F32, kind="ExternalOutput").ap()
    h_wqkv = dt_("wqkv_t", [128, 3, 8, D], F32, kind="ExternalInput").ap()
    h_wcv = dt_("wcv_r", [128, 4, 4, 2, 256], F32, kind="ExternalInput").ap()
    h_wb = dt_("wb_t", [128, 4, 8, D], F32, kind="ExternalInput").ap()
    h_wu = dt_("wu_t", [128, 2, 16, D], F32, kind="ExternalInput").ap()
    h_fw = dt_("fw_r", [128, 8, D], F32, kind="ExternalInput").ap()
    h_gattn = dt_("gattn_r", [128, 8], F32, kind="ExternalInput").ap()
    h_battn = dt_("battn_r", [128, 8], F32, kind="ExternalInput").ap()
    h_gconv = dt_("gconv_r", [128, 8], F32, kind="ExternalInput").ap()
    h_bconv = dt_("bconv_r", [128, 8], F32, kind="ExternalInput").ap()
    h_gb_bc = dt_("gb_bc", [128, D], F32, kind="ExternalInput").ap()
    h_bb_bc = dt_("bb_bc", [128, D], F32, kind="ExternalInput").ap()
    h_bbr = dt_("bb_r", [128, 32], F32, kind="ExternalInput").ap()
    h_rows = {n: dt_(n, [1, D], F32, kind="ExternalInput").ap()
              for n in ("bq", "bk", "bv", "cb", "bu", "fb")}
    h_U = dt_("Utri", [128, 128], BF16, kind="ExternalInput").ap()
    h_id = dt_("ident", [128, 128], BF16, kind="ExternalInput").ap()
    h_ones = dt_("ones1", [128, 128], BF16, kind="ExternalInput").ap()
    h_stab = dt_("stab", [128, NT], F32, kind="ExternalInput").ap()
    h_mask = dt_("mask", [128, 1], F32, kind="ExternalInput").ap()
    h_imask = dt_("imask", [128, 1], F32, kind="ExternalInput").ap()
    h_c75 = dt_("c75", [128, 1], F32, kind="ExternalInput").ap()
    h_zmg = dt_("zmg", [128, 24], F32, kind="ExternalInput").ap()
    d_wb = dt_("d_wb", [128, 4, 8, D], BF16).ap()
    d_wu = dt_("d_wu", [128, 2, 16, D], BF16).ap()
    h_e3 = dt_("e3neg", [1, 384], BF16, kind="ExternalInput").ap()

    import contextlib
    ctx = contextlib.ExitStack()
    sb = lambda n, sh, dt=F32: ctx.enter_context(nc.sbuf_tensor(n, sh, dt))
    psf = lambda n, sh: ctx.enter_context(nc.psum_tensor(n, sh, F32))
    sem = lambda n: ctx.enter_context(nc.semaphore(n))

    s_wqkv = sb("s_wqkv", [128, 3, 8, D], BF16)
    s_wcv = sb("s_wcv", [128, 4, 4, 2, 256], BF16)
    s_wdT = sb("s_wdT", [128, 8, D], BF16)
    s_ar = sb("s_ar", [128, 8, D], BF16)
    s_gbbc = sb("s_gbbc", [128, D], F32)
    s_bbbc = sb("s_bbbc", [128, D], F32)
    s_gelu = sb("s_gelu", [128, 32, 512], BF16)
    s_hbT = sb("s_hbT", [128, 8, 512], BF16)
    s_xP = [sb("s_x0", [128, D], F32), sb("s_x1", [128, D], F32)]
    s_xnP = [sb("s_xn0", [128, D], F32), sb("s_xn1", [128, D], F32)]
    s_a = sb("s_a", [128, D], F32)
    s_hstP = [sb("s_hst0", [128, D], BF16), sb("s_hst1", [128, D], BF16)]
    s_ltP = [sb("s_lt0", [128, 8, 128], BF16), sb("s_lt1", [128, 8, 128], BF16)]
    s_hcT = sb("s_hcT", [128, 8, 131], BF16)
    s_q = sb("s_q", [128, D], BF16)
    s_k = sb("s_k", [128, D], BF16)
    s_v = sb("s_v", [128, D], BF16)
    S1 = sb("S1", [128, D], F32)
    S2 = sb("S2", [128, D], F32)
    S3 = sb("S3", [128, D], F32)
    s_car = sb("s_car", [128, D], BF16)
    s_cq = s_car[0:1, :]
    s_cm = s_car[32:33, :]
    s_b3a = sb("s_b3a", [128, D], BF16)
    s_b3b = sb("s_b3b", [128, D], BF16)
    s_rows = {"bq": s_b3a[0:1, :], "bk": s_b3a[32:33, :], "bv": s_b3a[64:65, :],
              "cb": s_b3b[0:1, :], "bu": s_b3b[32:33, :], "fb": s_b3b[64:65, :]}
    s_bbr = sb("s_bbr", [128, 32], F32)
    s_gattn = sb("s_gattn", [128, 8], F32)
    s_battn = sb("s_battn", [128, 8], F32)
    s_battnb = sb("s_battnb", [128, 8], BF16)
    s_gconv = sb("s_gconv", [128, 8], F32)
    s_bconv = sb("s_bconv", [128, 8], F32)
    s_bconvb = sb("s_bconvb", [128, 8], BF16)
    s_U = sb("s_U", [128, 128], BF16)
    s_id = sb("s_id", [128, 128], BF16)
    s_ones = sb("s_ones", [128, 128], BF16)
    s_e3 = sb("s_e3", [1, 384], BF16)
    s_stab = sb("s_stab", [128, NT], F32)
    s_srec = sb("s_srec", [128, NT], F32)
    s_mask = sb("s_mask", [128, 1], F32)
    s_imask = sb("s_imask", [128, 1], F32)
    s_c75 = sb("s_c75", [128, 1], F32)
    s_rc75 = sb("s_rc75", [128, 1], F32)
    s_st = sb("s_st", [128, 8], F32)
    s_amax = sb("s_amax", [128, 32], F32)

    pf0 = s_gelu[0:1, 0:6, :].rearrange("p a b -> p (a b)")
    pf32 = s_hbT[0:1, 0:6, :].rearrange("p a b -> p (a b)")

    s_x, s_xn = s_xP[0], s_xnP[0]
    s_hst, s_lt = s_hstP[0], s_ltP[0]
    qz_g4 = s_gelu[:, 0:2, :].rearrange("p a b -> p (a b)")
    qz_zf = s_gelu[:, 2:4, :].rearrange("p a b -> p (a b)")
    qz_st = s_gelu[:, 4:6, :].rearrange("p a b -> p (a b)")

    P_A = psf("P_A", [128, D])
    P_B = psf("P_B", [128, D])
    P_C = psf("P_C", [128, D])
    P_D = psf("P_D", [128, 512])
    P_T = ctx.enter_context(nc.psum_tensor("P_T", [128, D], BF16))

    sY = sem("sY")
    sG = sem("sG")
    sC = sem("sC")
    sA = sem("sA")
    sT = sem("sT")
    sV = sem("sV")
    sS = sem("sS")

    r = Rec()
    E = r.emit

    def y_dma(dst, src):
        r.wait("sync", sY, r.cur(sY))
        v = r.inc(sY, 16)
        E("sync", lambda e, d=dst, s=src: e.dma_start(out=d, in_=s).then_inc(sY, 16))
        return v

    def g_dma(dst, src):
        r.wait("gpsimd", sG, r.cur(sG))
        v = r.inc(sG, 16)
        E("gpsimd", lambda e, d=dst, s=src: e.dma_start(out=d, in_=s).then_inc(sG, 16))
        return v

    def c_dma(dst, src):
        r.wait("sync", sC, r.cur(sC))
        v = r.inc(sC, 16)
        E("sync", lambda e, d=dst, s=src: e.dma_start(out=d, in_=s).then_inc(sC, 16))
        return v

    def a_dma(dst, src):
        r.wait("gpsimd", sA, r.cur(sA))
        v = r.inc(sA, 16)
        E("gpsimd", lambda e, d=dst, s=src: e.dma_start(out=d, in_=s).then_inc(sA, 16))
        return v

    def t_mm(out, lhsT, rhs, start, stop, inc=False):
        if inc:
            v = r.inc(sT)
            E("tensor", lambda e, o=out, l=lhsT, rr=rhs, st=start, sp=stop:
              e.matmul(o, l, rr, start=st, stop=sp).then_inc(sT, 1))
            return v
        E("tensor", lambda e, o=out, l=lhsT, rr=rhs, st=start, sp=stop:
          e.matmul(o, l, rr, start=st, stop=sp))
        return None

    def t_tr(out, in_, inc=False):
        if inc:
            v = r.inc(sT)
            E("tensor", lambda e, o=out, i=in_: e.transpose(o, i, s_id[:, :]).then_inc(sT, 1))
            return v
        E("tensor", lambda e, o=out, i=in_: e.transpose(o, i, s_id[:, :]))
        return None

    def v_op(fn, inc=False):
        if inc:
            v = r.inc(sV)
            E("vector", lambda e, f=fn: f(e).then_inc(sV, 1))
            return v
        E("vector", fn)
        return None

    def s_op(fn, inc=False):
        if inc:
            v = r.inc(sS)
            E("scalar", lambda e, f=fn: f(e).then_inc(sS, 1))
            return v
        E("scalar", fn)
        return None

    def tr_batch(srcb):
        for dc in range(7):
            t_tr(P_T[:, dc * 128:(dc + 1) * 128], srcb[:, dc * 128:(dc + 1) * 128])
        return t_tr(P_T[:, 7 * 128:8 * 128], srcb[:, 7 * 128:8 * 128], inc=True)

    last_stats_S = [0]

    def ln_stats(src, st, dump, wait_v=0, wait_y=0, wait_t=0):
        if wait_y:
            r.wait("scalar", sY, wait_y)
        if wait_v:
            r.wait("scalar", sV, wait_v)
        if wait_t:
            r.wait("scalar", sT, wait_t)
        s_op(lambda e: e.activation(dump[:, :], src, AF.Identity, accum_out=st[:, 0:1]))
        m = s_op(lambda e: e.activation(dump[:, :], src, AF.Square, accum_out=st[:, 1:2]), inc=True)
        last_stats_S[0] = m
        r.wait("vector", sS, m)
        v_op(lambda e: e.tensor_single_scalar(st[:, 2:3], st[:, 0:1], 1.0 / D, op=OP.mult))
        v_op(lambda e: e.tensor_single_scalar(st[:, 3:4], st[:, 1:2], 1.0 / D, op=OP.mult))
        v_op(lambda e: e.tensor_tensor(st[:, 4:5], st[:, 2:3], st[:, 2:3], op=OP.mult))
        v_op(lambda e: e.tensor_tensor(st[:, 3:4], st[:, 3:4], st[:, 4:5], op=OP.subtract))
        m = v_op(lambda e: e.tensor_single_scalar(st[:, 3:4], st[:, 3:4], EPS, op=OP.add), inc=True)
        r.wait("scalar", sV, m)
        m = s_op(lambda e: e.sqrt(st[:, 4:5], st[:, 3:4]), inc=True)
        r.wait("vector", sS, m)
        return v_op(lambda e: e.reciprocal(st[:, 5:6], st[:, 4:5]), inc=True)

    def ln_norm(out, src, st):
        return v_op(lambda e:
                    e.tensor_scalar(out, src, st[:, 2:3], st[:, 5:6], OP.subtract, OP.mult), inc=True)

    def quantize(dst_bf, src_f32, g4b, zfb, stepb):
        src3 = src_f32.rearrange("p (b w) -> p b w", w=32)
        v_op(lambda e, s3=src3: e.tensor_reduce(s_amax[:, :], s3, axis=AX.X, op=OP.max,
                                                apply_absolute_value=True))
        am3 = s_amax[:, :].rearrange("p (b o) -> p b o", o=1).broadcast_to([128, 32, 32])
        S13 = S1[:, :].rearrange("p (b w) -> p b w", w=32)
        v_op(lambda e, a=am3, o=S13: e.tensor_copy(o, a))
        v_op(lambda e: e.tensor_single_scalar(S1[:, :], S1[:, :], 1.0 / 7.5, op=OP.mult))
        v_op(lambda e: e.tensor_single_scalar(S1[:, :], S1[:, :], 1e-30, op=OP.max))
        v_op(lambda e: e.reciprocal(S2[:, :], S1[:, :]))
        v_op(lambda e: e.tensor_tensor(S3[:, :], src_f32, S2[:, :], op=OP.mult))
        v_op(lambda e: e.tensor_single_scalar(S2[:, :], S3[:, :], -1.0, op=OP.mult))
        v_op(lambda e: e.tensor_tensor(S2[:, :], S2[:, :], S3[:, :], op=OP.max))
        v_op(lambda e: e.tensor_single_scalar(g4b[:, :], S2[:, :], 4.0, op=OP.is_ge))
        v_op(lambda e: e.tensor_single_scalar(S2[:, :], S2[:, :], 2.0, op=OP.is_ge))
        v_op(lambda e: e.tensor_scalar(stepb[:, :], S2[:, :], 0.125, 0.125, OP.mult, OP.add))
        v_op(lambda e: e.scalar_tensor_tensor(stepb[:, :], g4b[:, :], 0.25, stepb[:, :],
                                              op0=OP.mult, op1=OP.add))
        v_op(lambda e: e.tensor_scalar(S2[:, :], S2[:, :], -4.0, 8.0, OP.mult, OP.add))
        v_op(lambda e: e.scalar_tensor_tensor(S2[:, :], g4b[:, :], -2.0, S2[:, :],
                                              op0=OP.mult, op1=OP.add))
        v_op(lambda e: e.tensor_tensor(S3[:, :], S3[:, :], S2[:, :], op=OP.mult))
        v_op(lambda e: e.tensor_single_scalar(S3[:, :], S3[:, :], 0.5, op=OP.subtract))
        v_op(lambda e: e.tensor_copy(S2[:, :].bitcast(I32), S3[:, :]))
        v_op(lambda e: e.tensor_copy(zfb[:, :], S2[:, :].bitcast(I32)))
        v_op(lambda e: e.tensor_tensor(S2[:, :], zfb[:, :], S3[:, :], op=OP.is_lt))
        v_op(lambda e: e.tensor_tensor(S2[:, :], zfb[:, :], S2[:, :], op=OP.add))
        v_op(lambda e: e.tensor_tensor(S2[:, :], S2[:, :], stepb[:, :], op=OP.mult))
        v_op(lambda e: e.tensor_tensor(S3[:, :], S2[:, :], S1[:, :], op=OP.mult))
        return v_op(lambda e, o=dst_bf: e.tensor_copy(o, S3[:, :]), inc=True)

    for dst, src in ((s_U, h_U), (s_id, h_id), (s_ones, h_ones), (s_e3, h_e3),
                     (s_stab, h_stab), (s_mask, h_mask), (s_imask, h_imask), (s_c75, h_c75),
                     (s_gattn, h_gattn), (s_battn, h_battn), (s_gconv, h_gconv),
                     (s_bconv, h_bconv), (s_gbbc, h_gb_bc), (s_bbbc, h_bb_bc),
                     (s_bbr, h_bbr)):
        y_dma(dst[...], src[...])
    for n in ("bq", "bk", "bv", "cb", "bu", "fb"):
        g_bias_last = g_dma(s_rows[n], h_rows[n][:, :])
    g_zmg = g_dma(s_hcT[:, :, 0:3], h_zmg[:, :].rearrange("p (a b) -> p a b", b=3))
    g_wq = g_dma(s_wqkv[...], h_wqkv[...])
    g_wc = g_dma(s_wcv[...], h_wcv[...])
    consts_y = r.cur(sY)

    r.wait("vector", sY, consts_y)
    v_op(lambda e: e.reciprocal(s_srec[:, :], s_stab[:, :]))
    rc_m = v_op(lambda e: e.reciprocal(s_rc75[:, :], s_c75[:, :]), inc=True)
    r.wait("vector", sG, r.cur(sG))
    bat = v_op(lambda e: e.tensor_copy(s_battnb[:, :], s_battn[:, :]), inc=True)
    bcv = v_op(lambda e: e.tensor_copy(s_bconvb[:, :], s_bconv[:, :]), inc=True)

    r.wait("tensor", sV, bcv)
    bias_base = {"bq": 0, "bk": 32, "bv": 64, "cb": 0, "bu": 32, "fb": 64}
    for p, name in enumerate(("bq", "bk", "bv")):
        dst = s_rows[name]
        bp = bias_base[name]
        for half in range(2):
            for k in range(8):
                t_mm(P_D[0:1, :], s_battnb[:, k:k + 1], s_wqkv[:, p, k, half * 512:(half + 1) * 512],
                     start=(k == 0), stop=False)
            mdone = t_mm(P_D[0:1, :], s_ones[bp:bp + 1, 0:1], dst[0:1, half * 512:(half + 1) * 512],
                         start=False, stop=True, inc=True)
            r.wait("vector", sT, mdone)
            ev = v_op(lambda e, h=half: e.tensor_copy(s_car[0:1, h * 512:(h + 1) * 512], P_D[0:1, :]), inc=True)
            r.wait("tensor", sV, ev)
        if bp == 0:
            cp = v_op(lambda e, d=dst: e.tensor_copy(d, s_car[0:1, :]), inc=True)
            r.wait("tensor", sV, cp)
        else:
            r.wait("sync", sV, ev)
            yv = y_dma(dst, s_car[0:1, :])
            r.wait("vector", sY, yv)
            r.wait("tensor", sY, yv)
    for g in range(4):
        for tau in range(4):
            for kc in range(2):
                t_mm(P_D[0:1, 0:256], s_bconvb[:, g * 2 + kc:g * 2 + kc + 1], s_wcv[:, g, tau, kc, :],
                     start=(tau == 0 and kc == 0), stop=False)
        mdone = t_mm(P_D[0:1, 0:256], s_ones[0:1, 0:1], s_rows["cb"][0:1, g * 256:(g + 1) * 256],
                     start=False, stop=True, inc=True)
        r.wait("vector", sT, mdone)
        ev = v_op(lambda e, gg=g: e.tensor_copy(s_rows["cb"][0:1, gg * 256:(gg + 1) * 256], P_D[0:1, 0:256]), inc=True)
        r.wait("tensor", sV, ev)

    for s in range(3):
        for hh in range(2):
            pfd = None
            for g in (0, 1) if hh == 0 else (2, 3):
                first_mm = True
                for kc in range(2):
                    for tau in range(0, 3 - s):
                        pfd = t_mm(P_D[0:1, (g % 2) * 256:(g % 2) * 256 + 256],
                                   s_bconvb[:, g * 2 + kc:g * 2 + kc + 1], s_wcv[:, g, tau, kc, :],
                                   start=first_mm, stop=(kc == 1 and tau == 2 - s),
                                   inc=(g % 2 == 1 and kc == 1 and tau == 2 - s))
                        first_mm = False
            r.wait("vector", sT, pfd)
            ev = v_op(lambda e, ss=s, h2=hh: e.tensor_copy(
                pf0[0:1, ss * 1024 + h2 * 512:ss * 1024 + (h2 + 1) * 512], P_D[0:1, :]), inc=True)
            r.wait("tensor", sV, ev)
    pf_m = v_op(lambda e: e.tensor_scalar(pf32[0:1, :], pf0[0:1, :], s_imask[0:1, 0:1], None, OP.mult), inc=True)

    r.wait("vector", sT, r.cur(sT))
    for p in range(3):
        for k in range(8):
            v_op(lambda e, pp=p, kk=k: e.tensor_scalar(
                s_wqkv[:, pp, kk, :], s_wqkv[:, pp, kk, :], s_gattn[:, kk:kk + 1], None, OP.mult))
    for g in range(4):
        for kc in range(2):
            v_op(lambda e, gg=g, kk=kc: e.tensor_scalar(
                s_wcv[:, gg, :, kk, :], s_wcv[:, gg, :, kk, :], s_gconv[:, gg * 2 + kk:gg * 2 + kk + 1],
                None, OP.mult))

    r.wait("vector", sV, rc_m)
    for oc in range(8):
        ly = y_dma(s_xn[:, :], h_fw[:, oc, :])
        r.wait("vector", sY, ly)
        qd = quantize(s_hst[:, :], s_xn[:, :], qz_g4, qz_zf, qz_st)
        r.wait("tensor", sV, qd)
        td = tr_batch(s_hst)
        r.wait("vector", sT, td)
        ev = v_op(lambda e, o=oc: e.tensor_copy(
            s_wdT[:, :, o * 128:(o + 1) * 128], P_T[:, :].rearrange("p (a b) -> p a b", b=128)), inc=True)
        r.wait("sync", sV, ev)

    if _STAGE == 9:
        E("gpsimd", lambda e: e.wait_ge(sV, r.cur(sV)))
        for k in range(8):
            g_dma(o_out[k * 128:(k + 1) * 128, :], s_wdT[:, k, :])
        g_dma(o_out[1024:1152, :], s_hst[:, :])
        g_dma(o_out[1152:1280, :], s_xn[:, :])
        E("gpsimd", lambda e: e.wait_ge(sG, r.cur(sG)))
        with ctx:
            with nc.Block() as block:
                for eng in ENGS:
                    def runner(e, ops=tuple(r.ops[eng])):
                        for f in ops:
                            f(e)
                    getattr(block, eng)(runner)
        return nc

    if _STAGE == 7:
        r.wait("gpsimd", sV, pf_m)
        g_dma(o_out[0:3, :].rearrange("(a p) b -> p (a b)", p=1), pf0[0:1, :])
        g_dma(o_out[4:7, :].rearrange("(a p) b -> p (a b)", p=1), pf32[0:1, :])
        E("gpsimd", lambda e: e.wait_ge(sG, r.cur(sG)))
        with ctx:
            with nc.Block() as block:
                for eng in ENGS:
                    def runner(e, ops=tuple(r.ops[eng])):
                        for f in ops:
                            f(e)
                    getattr(block, eng)(runner)
        return nc

    startup_V = r.cur(sV)

    arena_flat = s_ar[:, :, :].rearrange("p a b -> p (a b)")
    scrY = 0
    for i, (srcap, dstap) in enumerate(
        [(h_wb[:, qt, :, eh * 512:(eh + 1) * 512], d_wb[:, qt, :, eh * 512:(eh + 1) * 512])
         for qt in range(4) for eh in range(2)] +
        [(h_wu[:, hf, q4 * 4:(q4 + 1) * 4, :], d_wu[:, hf, q4 * 4:(q4 + 1) * 4, :])
         for hf in range(2) for q4 in range(4)]):
        half = i % 2
        hv = arena_flat[:, half * 4096:(half + 1) * 4096]
        shp = srcap.shape
        hvv = hv.rearrange("p (a b) -> p a b", a=shp[1])
        if i >= 2:
            r.wait("gpsimd", sG, scr_st[i - 2])
        ld = g_dma(hvv, srcap)
        r.wait("gpsimd", sG, ld)
        stv = g_dma(dstap, hvv)
        if i == 0:
            scr_st = {}
        scr_st[i] = stv
    scratch_done_G = r.cur(sG)

    P2 = lambda v: {0: v, 1: v}
    pst = P2(0)
    m_x_freeV = P2(r.cur(sV))
    m_x_statS = 0
    m_xn_statS = 0
    m_a_statS = 0
    m_hst_T = P2(r.cur(sT))
    m_st_V = startup_V
    carry_dma_q = 0
    carry_dma_m = 0
    m_PA_free = startup_V
    m_PB_free = startup_V
    m_PC_free = startup_V
    m_PT_free = r.cur(sV)
    m_hcT_used = r.cur(sT)
    m_quant_prev = startup_V
    m_boomstoreY = 0
    grp_pout_y = []
    grp_hbT = []
    arena_free_T = r.cur(sT)
    TM = {}
    dbg_blockG = [0]

    def emit_front(til):
        nonlocal carry_dma_q, carry_dma_m, m_PA_free, m_PB_free, m_PC_free
        nonlocal m_PT_free, m_hcT_used, m_x_statS, m_xn_statS, m_a_statS, m_st_V
        p = til % 2
        X, XN = s_xP[p], s_xnP[p]
        H, LT = s_hstP[p], s_ltP[p]
        own = til >= NT_LITE
        tj = til - NT_LITE
        first = til == 0

        if _SERIAL:
            for _eng in ENGS:
                for _sem in (sY, sG, sT, sV, sS, sC, sA):
                    r.wait(_eng, _sem, r.cur(_sem))

        r.wait("sync", sY, pst[p])
        r.wait("sync", sV, m_x_freeV[p])
        r.wait("sync", sS, m_x_statS)
        lx = y_dma(X[:, :], x_all[til * T:(til + 1) * T, :])

        ln_stats(X[:, :], s_st, H, wait_v=m_st_V, wait_y=max(lx, m_boomstoreY),
                 wait_t=m_hst_T[p])
        m_x_statS = last_stats_S[0]
        r.wait("vector", sS, m_xn_statS)
        r.wait("vector", sY, pst[p]) if _STAGE in (1, 2) else None
        xn_m = ln_norm(XN[:, :], X[:, :], s_st)
        x_free = xn_m

        ln_stats(XN[:, :], s_st, H, wait_v=xn_m)
        m_xn_statS = last_stats_S[0]
        r.wait("vector", sT, m_hst_T[p])
        hc_m = ln_norm(H[:, :], XN[:, :], s_st)

        if not first:
            r.wait("vector", sT, m_hcT_used)
            v_op(lambda e: e.tensor_copy(s_hcT[:, :, 0:3], s_hcT[:, :, 128:131]))
        else:
            r.wait("tensor", sG, r.cur(sG))
        r.wait("tensor", sV, hc_m)
        r.wait("tensor", sV, m_PT_free)
        tr_m = tr_batch(H)
        m_hst_T[p] = tr_m
        r.wait("vector", sT, tr_m)
        ev_m = v_op(lambda e: e.tensor_copy(s_hcT[:, :, 3:131],
                                            P_T[:, :].rearrange("p (a b) -> p a b", b=128)), inc=True)
        m_PT_free = ev_m

        r.wait("tensor", sV, ev_m)
        r.wait("tensor", sV, m_PB_free)
        for g in range(4):
            for tau in range(4):
                for kc in range(2):
                    t_mm(P_B[:, g * 256:(g + 1) * 256], s_hcT[:, g * 2 + kc, tau:tau + 128],
                         s_wcv[:, g, tau, kc, :],
                         start=(g in (0, 2) and tau == 0 and kc == 0), stop=False)
        if til == 0 or til == NT_LITE:
            pfx = pf0 if til == 0 else pf32
            for s in range(3):
                t_mm(P_B[:, 0:512], s_e3[0:1, s * 128:(s + 1) * 128],
                     pfx[0:1, s * 1024:s * 1024 + 512], start=False, stop=False)
                t_mm(P_B[:, 512:1024], s_e3[0:1, s * 128:(s + 1) * 128],
                     pfx[0:1, s * 1024 + 512:(s + 1) * 1024], start=False, stop=False)
        t_mm(P_B[:, 0:512], s_ones[0:1, :], s_rows["cb"][0:1, 0:512], start=False, stop=True)
        conv_m = t_mm(P_B[:, 512:1024], s_ones[0:1, :], s_rows["cb"][0:1, 512:1024],
                      start=False, stop=True, inc=True)
        m_hcT_used = conv_m

        r.wait("vector", sT, conv_m)
        r.wait("vector", sS, m_a_statS)
        a1_m = v_op(lambda e: e.tensor_tensor(s_a[:, :], XN[:, :], P_B[:, :], op=OP.add), inc=True)

        ln_stats(s_a[:, :], s_st, H, wait_v=a1_m)
        m_a_statS = last_stats_S[0]
        r.wait("vector", sT, m_hst_T[p])
        ha_m = ln_norm(H[:, :], s_a[:, :], s_st)
        if _STAGE == 6 and til == NT_LITE:
            r.wait("sync", sV, ha_m)
            y_dma(o_out[3456:3584, 0:8], s_st[:, :])

        r.wait("tensor", sV, ha_m)
        r.wait("tensor", sV, m_PT_free)
        tr_m = tr_batch(H)
        m_hst_T[p] = tr_m
        r.wait("vector", sT, tr_m)
        haT_m = v_op(lambda e: e.tensor_copy(LT[:, :, :],
                                             P_T[:, :].rearrange("p (a b) -> p a b", b=128)), inc=True)
        m_PT_free = haT_m

        r.wait("tensor", sV, haT_m)
        r.wait("tensor", sV, a1_m)
        projs = [(0, s_q, "bq"), (1, s_k, "bk")] + ([(2, s_v, "bv")] if own else [])
        pe_free = 0
        ev_q = 0
        for pi, dstb, bn in projs:
            bias = s_rows[bn]
            bp = {"bq": 0, "bk": 32, "bv": 64}[bn]
            if pe_free:
                r.wait("tensor", sV, pe_free)
            pj_m = None
            for half in range(2):
                for k in range(8):
                    t_mm(P_B[:, half * 512:(half + 1) * 512], LT[:, k, :],
                         s_wqkv[:, pi, k, half * 512:(half + 1) * 512],
                         start=(k == 0), stop=False)
                pj_m = t_mm(P_B[:, half * 512:(half + 1) * 512], s_ones[bp:bp + 1, :],
                            bias[0:1, half * 512:(half + 1) * 512], start=False, stop=True,
                            inc=(half == 1))
            r.wait("vector", sT, pj_m)
            pe_free = v_op(lambda e, d=dstb: e.tensor_copy(d[:, :], P_B[:, :]), inc=True)
            if pi == 0:
                ev_q = pe_free
        m_PB_free = pe_free

        if _STAGE == 6 and til in (0, 4, 8, 12, 16, 20, 24, 28, 30, 31):
            slot = (0, 4, 8, 12, 16, 20, 24, 28, 30, 31).index(til)
            r.wait("sync", sV, pe_free)
            y_dma(o_out[256 + slot * 256:384 + slot * 256, :], X[:, :])
            y_dma(o_out[384 + slot * 256:512 + slot * 256, :], s_a[:, :])

        if _STAGE == 6 and til == NT_LITE:
            dq = v_op(lambda e: e.tensor_copy(S2[:, :], s_q[:, :]), inc=True)
            r.wait("sync", sV, dq)
            y_dma(o_out[2560:2688, :], S2[:, :])
            r.wait("sync", sV, pe_free)
            r.wait("gpsimd", sV, pe_free)
            g_dma(o_out[2816:2944, :], s_q[:, :])
            g_dma(o_out[2944:3072, :], s_k[:, :])
            g_dma(o_out[3072:3200, :], s_v[:, :])
            g_dma(o_out[3200:3328, :], H[:, :])
            y_dma(o_out[3328:3456, :], s_a[:, :])
            g_dma(o_out[3584:3712, :], LT[:, :, :].rearrange("p a b -> p (a b)"))
            for kc in range(8):
                g_dma(o_out[1024 + kc * 128:1152 + kc * 128, :], s_wqkv[:, 0, kc, :])
            g_dma(o_out[1020:1021, :], s_rows["bq"])
            g_dma(o_out[1021:1022, :], s_rows["bk"])
            g_dma(o_out[1022:1023, 0:128], s_ones[0:1, 0:128])
            dbg_blockG[0] = r.cur(sG)

        r.wait("tensor", sV, ev_q)
        r.wait("tensor", sC, max(carry_dma_q, carry_dma_m))
        r.wait("tensor", sV, m_PA_free)
        csq_m = None
        for half in range(2):
            csq_m = t_mm(P_A[:, half * 512:(half + 1) * 512], s_U[:, :],
                         s_q[:, half * 512:(half + 1) * 512],
                         start=True, stop=first, inc=(first and half == 1))
        if not first:
            for half in range(2):
                csq_m = t_mm(P_A[:, half * 512:(half + 1) * 512], s_ones[0:1, :],
                             s_cq[0:1, half * 512:(half + 1) * 512], start=False, stop=True,
                             inc=(half == 1))

        r.wait("vector", sT, csq_m)
        if _STAGE == 6 and til == NT_LITE:
            dbg = v_op(lambda e: e.tensor_copy(S1[:, :], P_A[:, :]), inc=True)
            r.wait("sync", sV, dbg)
            y_dma(o_out[0:128, :], S1[:, :])
        if _STAGE == 6 and til == NT_LITE - 1:
            dbg = v_op(lambda e: e.tensor_copy(S1[:, :], P_A[:, :]), inc=True)
            r.wait("sync", sV, dbg)
            y_dma(o_out[128:256, :], S1[:, :])
        r.wait("vector", sC, carry_dma_q)
        stq = v_op(lambda e: e.tensor_copy(s_b3a[96:128, :], P_A[96:128, :]), inc=True)
        r.wait("sync", sV, stq)
        carry_dma_q = c_dma(s_cq[0:1, :], s_b3a[127:128, :])
        if _STAGE == 6 and til == NT_LITE:
            r.wait("vector", sG, dbg_blockG[0])
        m_m = v_op(lambda e: e.scalar_tensor_tensor(
            s_q[:, :], P_A[:, :], s_srec[:, til:til + 1], s_k[:, :], op0=OP.mult, op1=OP.mult), inc=True)
        m_PA_free = m_m

        r.wait("tensor", sV, m_m)
        r.wait("tensor", sV, m_PC_free)
        csm_m = None
        for half in range(2):
            csm_m = t_mm(P_C[:, half * 512:(half + 1) * 512], s_U[:, :],
                         s_q[:, half * 512:(half + 1) * 512],
                         start=True, stop=first, inc=(first and half == 1))
        if not first:
            for half in range(2):
                csm_m = t_mm(P_C[:, half * 512:(half + 1) * 512], s_ones[32:33, :],
                             s_cm[0:1, half * 512:(half + 1) * 512], start=False, stop=True,
                             inc=(half == 1))

        r.wait("vector", sT, csm_m)
        r.wait("vector", sC, carry_dma_m)
        cm_row = v_op(lambda e: e.tensor_copy(s_b3b[96:128, :], P_C[96:128, :]), inc=True)
        r.wait("sync", sV, cm_row)
        carry_dma_m = c_dma(s_cm[0:1, :], s_b3b[127:128, :])

        if own and _STAGE != 1:
            t1_m = v_op(lambda e: e.scalar_tensor_tensor(
                S2[:, :], P_C[:, :], s_srec[:, til:til + 1], s_v[:, :], op0=OP.mult, op1=OP.mult), inc=True)
            m_PC_free = t1_m
            a2_m = v_op(lambda e: e.tensor_tensor(s_a[:, :], s_a[:, :], S2[:, :], op=OP.add), inc=True)
        else:
            m_PC_free = cm_row
            a2_m = cm_row

        if til == NT_LITE - 1:
            r.wait("vector", sC, max(carry_dma_q, carry_dma_m))
            v_op(lambda e: e.tensor_scalar(s_cq[0:1, :], s_cq[0:1, :], s_mask[0:1, 0:1], None, OP.mult))
            mk = v_op(lambda e: e.tensor_scalar(s_cm[0:1, :], s_cm[0:1, :], s_mask[32:33, 0:1], None, OP.mult), inc=True)
            r.wait("tensor", sV, mk)

        if not own:
            m_st_V = ha_m
            m_x_freeV[p] = x_free
            return

        if _STAGE == 6:
            m_st_V = ha_m
            m_x_freeV[p] = x_free
            m_quant_prev = a2_m
            return

        if _STAGE in (1, 2):
            r.wait("sync", sV, a2_m)
            pst[p] = y_dma(o_out[tj * T:(tj + 1) * T, :], s_a[:, :])
            m_st_V = ha_m
            m_x_freeV[p] = x_free
            m_quant_prev = a2_m
            return

        ln_stats(s_a[:, :], s_st, H, wait_v=a2_m)
        m_a_statS = last_stats_S[0]
        bn_m = ln_norm(S2[:, :], s_a[:, :], s_st)
        m_st_V = bn_m
        v_op(lambda e: e.tensor_tensor(S3[:, :], S2[:, :], s_gbbc[:, :], op=OP.mult))
        r.wait("vector", sT, m_hst_T[p])
        hb_m = v_op(lambda e: e.tensor_tensor(H[:, :], S3[:, :], s_bbbc[:, :], op=OP.add), inc=True)

        r.wait("tensor", sV, hb_m)
        r.wait("tensor", sV, m_PT_free)
        tr_m = tr_batch(H)
        m_hst_T[p] = tr_m
        r.wait("vector", sT, tr_m)
        toff = (tj % GRP) * 128
        ev_m = v_op(lambda e, to=toff: e.tensor_copy(
            s_hbT[:, :, to:to + 128], P_T[:, :].rearrange("p (a b) -> p a b", b=128)), inc=True)
        m_PT_free = ev_m
        grp_hbT.append(ev_m)

        paX = v_op(lambda e: e.tensor_tensor(X[:, :], X[:, :], s_a[:, :], op=OP.add), inc=True)
        m_x_freeV[p] = paX
        TM[til] = dict(paX=paX)

    def emit_back(til):
        nonlocal m_PA_free, m_PB_free, m_PC_free, m_PT_free, m_quant_prev
        nonlocal m_boomstoreY, arena_free_T, grp_pout_y, grp_hbT
        own = til >= NT_LITE
        if not own or _STAGE in (1, 2, 6):
            return
        p = til % 2
        X, XN = s_xP[p], s_xnP[p]
        H, LT = s_hstP[p], s_ltP[p]
        tj = til - NT_LITE

        r.wait("vector", sY, m_boomstoreY)
        r.wait("vector", sT, max(m_hst_T[p], arena_free_T))
        qd_m = quantize(H[:, :], XN[:, :], qz_g4, qz_zf, qz_st)
        m_quant_prev = qd_m

        r.wait("tensor", sV, qd_m)
        r.wait("tensor", sV, m_PT_free)
        tr_m = tr_batch(H)
        m_hst_T[p] = tr_m
        r.wait("vector", sT, tr_m)
        ev_m = v_op(lambda e: e.tensor_copy(LT[:, :, :],
                                            P_T[:, :].rearrange("p (a b) -> p a b", b=128)), inc=True)
        m_PT_free = ev_m

        r.wait("tensor", sV, ev_m)
        r.wait("tensor", sV, m_PA_free)
        F_m = None
        for k in range(8):
            for half in range(2):
                t_mm(P_A[:, half * 512:(half + 1) * 512], LT[:, k, :],
                     s_wdT[:, k, half * 512:(half + 1) * 512], start=(k == 0), stop=False)
        for half in range(2):
            F_m = t_mm(P_A[:, half * 512:(half + 1) * 512], s_ones[64:65, :],
                       s_rows["fb"][0:1, half * 512:(half + 1) * 512], start=False, stop=True,
                       inc=(half == 1))

        r.wait("vector", sT, F_m)
        F_ev = v_op(lambda e: e.tensor_tensor(X[:, :], X[:, :], P_A[:, :], op=OP.add), inc=True)
        m_PA_free = F_ev
        r.wait("sync", sV, F_ev)
        pst[p] = y_dma(o_out[tj * T:(tj + 1) * T, :], X[:, :])
        grp_pout_y.append(pst[p])

        if _STAGE == 3 or tj % GRP != GRP - 1:
            if tj % GRP == GRP - 1:
                grp_pout_y = []
                grp_hbT = []
            return

        g0 = tj - (GRP - 1)
        hbT_ready = max(grp_hbT)
        arA = arena_flat[:, 0:4096]
        arB = arena_flat[:, 4096:8192]
        b1v = [arA.rearrange("p (a b) -> p a b", b=512), arB.rearrange("p (a b) -> p a b", b=512)]
        b2v = [arA.rearrange("p (a b) -> p a b", b=1024), arB.rearrange("p (a b) -> p a b", b=1024)]
        pieces = [(qt, eh) for qt in range(4) for eh in range(2)]
        ldb = {}
        b1_done = {}
        gelu_evacs = []
        r.wait("gpsimd", sG, scratch_done_G)
        for i in range(2):
            r.wait("gpsimd", sT, arena_free_T)
            qt, eh = pieces[i]
            ldb[i] = a_dma(b1v[i % 2], d_wb[:, qt, :, eh * 512:(eh + 1) * 512])
        for i, (qt, eh) in enumerate(pieces):
            half = i % 2
            r.wait("tensor", sA, ldb[i])
            if i == 0:
                r.wait("tensor", sV, hbT_ready)
                r.wait("tensor", sV, m_PC_free)
                r.wait("tensor", sV, m_quant_prev)
            for e4 in range(4):
                ec = qt * 8 + eh * 4 + e4
                bank = ec % 2
                if len(gelu_evacs) >= 2:
                    r.wait("tensor", sS, gelu_evacs[-2])
                md = None
                for dc in range(8):
                    md = t_mm(P_C[:, bank * 512:(bank + 1) * 512],
                              b1v[half][:, dc, e4 * 128:(e4 + 1) * 128],
                              s_hbT[:, dc, :], start=(dc == 0), stop=(dc == 7),
                              inc=(dc == 7))
                b1_done[i] = md
                r.wait("scalar", sT, md)
                gv = s_op(lambda e, ee=ec, bk=bank: e.activation(
                    s_gelu[:, ee, :], P_C[:, bk * 512:(bk + 1) * 512], AF.Gelu,
                    bias=s_bbr[:, ee:ee + 1]), inc=True)
                gelu_evacs.append(gv)
            if i + 2 < 8:
                r.wait("gpsimd", sT, b1_done[i])
                qt2, eh2 = pieces[i + 2]
                ldb[i + 2] = a_dma(b1v[half], d_wb[:, qt2, :, eh2 * 512:(eh2 + 1) * 512])
        ldu = {}
        for i in range(2):
            r.wait("gpsimd", sT, b1_done[6 + i])
            hf, q4 = (0, i)
            ldu[i] = a_dma(b2v[i % 2], d_wu[:, hf, q4 * 4:(q4 + 1) * 4, :])
        r.wait("tensor", sV, m_PA_free)
        r.wait("tensor", sV, m_PB_free)
        r.wait("tensor", sS, gelu_evacs[-1])
        b2_done = []
        piece_last = {}
        for i8 in range(8):
            half = i8 % 2
            r.wait("tensor", sA, ldu[i8])
            for tt in range(GRP):
                for e4 in range(4):
                    ec = i8 * 4 + e4
                    for dh in range(2):
                        if tt < 3:
                            dst = (P_A, P_B, P_C)[tt][:, dh * 512:(dh + 1) * 512]
                        else:
                            dst = P_D[:, :] if dh == 0 else P_T[:, :].bitcast(F32)
                        mret = t_mm(dst, s_gelu[:, ec, tt * 128:(tt + 1) * 128],
                                    b2v[half][:, e4, dh * 512:(dh + 1) * 512],
                                    start=(i8 == 0 and e4 == 0), stop=False,
                                    inc=(tt == 3 and e4 == 3 and dh == 1))
                        if mret is not None:
                            piece_last[i8] = mret
            if i8 + 2 < 8:
                r.wait("gpsimd", sT, piece_last[i8])
                hf, q4 = divmod(i8 + 2, 4)
                ldu[i8 + 2] = a_dma(b2v[half], d_wu[:, hf, q4 * 4:(q4 + 1) * 4, :])
        for tt in range(GRP):
            md = None
            for dh in range(2):
                if tt < 3:
                    dst = (P_A, P_B, P_C)[tt][:, dh * 512:(dh + 1) * 512]
                else:
                    dst = P_D[:, :] if dh == 0 else P_T[:, :].bitcast(F32)
                md = t_mm(dst, s_ones[32:33, :], s_rows["bu"][0:1, dh * 512:(dh + 1) * 512],
                          start=False, stop=True, inc=(dh == 1))
            b2_done.append(md)
        arena_free_T = b2_done[-1]
        r.wait("sync", sY, grp_pout_y[-1])
        r.wait("sync", sV, m_quant_prev)
        final_stores = []
        for tt in range(GRP):
            tloc = g0 + tt
            dstS = S1 if tt % 2 == 0 else S3
            lb = y_dma(dstS[:, :], o_out[tloc * T:(tloc + 1) * T, :])
            r.wait("vector", sY, lb)
            r.wait("vector", sT, b2_done[tt])
            if tt < 3:
                pt = (P_A, P_B, P_C)[tt]
                fv = v_op(lambda e, sbb=dstS, pp=pt: e.tensor_tensor(
                    sbb[:, :], sbb[:, :], pp[:, :], op=OP.add), inc=True)
            else:
                v_op(lambda e, sbb=dstS: e.tensor_tensor(
                    sbb[:, 0:512], sbb[:, 0:512], P_D[:, :], op=OP.add))
                fv = v_op(lambda e, sbb=dstS: e.tensor_tensor(
                    sbb[:, 512:1024], sbb[:, 512:1024], P_T[:, :].bitcast(F32), op=OP.add), inc=True)
            r.wait("sync", sV, fv)
            final_stores.append(y_dma(o_out[tloc * T:(tloc + 1) * T, :], dstS[:, :]))
        m_boomstoreY = final_stores[-1]
        m_PA_free = r.cur(sV)
        m_PB_free = r.cur(sV)
        m_PC_free = r.cur(sV)
        grp_pout_y = []
        grp_hbT = []

    done_back = set()
    for til in range(NT):
        emit_front(til)
        if til - 1 >= 0 and (til - 1) not in done_back:
            emit_back(til - 1)
            done_back.add(til - 1)
        tj = til - NT_LITE
        if tj >= 0 and tj % GRP == GRP - 1:
            emit_back(til)
            done_back.add(til)

    r.wait("sync", sY, r.cur(sY))

    with ctx:
        with nc.Block() as block:
            for eng in ENGS:
                def runner(e, ops=tuple(r.ops[eng])):
                    for f in ops:
                        f(e)
                getattr(block, eng)(runner)
    return nc


_NC_CACHE = {}


def _get_nc(stage):
    stage = (stage, _SERIAL)
    if stage not in _NC_CACHE:
        global _STAGE
        _STAGE = stage[0]
        _NC_CACHE[stage] = build_nc()
    return _NC_CACHE[stage]


def _prep_shared(inputs):
    f32 = lambda a: np.ascontiguousarray(np.asarray(a, dtype=np.float32))
    Wq = f32(inputs["L0_attn_Wq"]); Wk = f32(inputs["L0_attn_Wk"]); Wv = f32(inputs["L0_attn_Wv"])
    wqkv = np.stack([W.T.reshape(8, 128, D).transpose(1, 0, 2) for W in (Wq, Wk, Wv)], axis=1)
    cw = f32(inputs["L0_attn_conv_w"]).reshape(4, 256, 2, 128, 4)
    wcv = np.ascontiguousarray(cw.transpose(3, 0, 4, 2, 1))
    Wb = f32(inputs["L0_attn_Wb"])
    wb_t = np.ascontiguousarray(Wb.T.reshape(8, 128, 4, 1024).transpose(1, 2, 0, 3))
    Wu = f32(inputs["L0_attn_Wu"])
    wu_t = np.ascontiguousarray(Wu.T.reshape(2, 16, 128, 1024).transpose(2, 0, 1, 3))
    fw = f32(inputs["L0_ffn_weight"]).reshape(8, 128, D).transpose(1, 0, 2)
    rvec = lambda v: np.ascontiguousarray(f32(v).reshape(8, 128).T)
    bfc = lambda a: np.ascontiguousarray(a.astype(ml_dtypes.bfloat16))
    return {
        "wqkv_t": np.ascontiguousarray(wqkv),
        "wcv_r": wcv,
        "wb_t": wb_t,
        "wu_t": wu_t,
        "fw_r": np.ascontiguousarray(fw),
        "gattn_r": rvec(inputs["L0_attn_ln_attn_g"]),
        "battn_r": rvec(inputs["L0_attn_ln_attn_b"]),
        "gconv_r": rvec(inputs["L0_attn_ln_conv_g"]),
        "bconv_r": rvec(inputs["L0_attn_ln_conv_b"]),
        "gb_bc": np.ascontiguousarray(np.broadcast_to(f32(inputs["L0_attn_ln_boom_g"]), (128, D))),
        "bb_bc": np.ascontiguousarray(np.broadcast_to(f32(inputs["L0_attn_ln_boom_b"]), (128, D))),
        "bb_r": np.ascontiguousarray(f32(inputs["L0_attn_bb"]).reshape(32, 128).T),
        "bq": f32(inputs["L0_attn_bq"]).reshape(1, D),
        "bk": f32(inputs["L0_attn_bk"]).reshape(1, D),
        "bv": f32(inputs["L0_attn_bv"]).reshape(1, D),
        "cb": f32(inputs["L0_attn_conv_b"]).reshape(1, D),
        "bu": f32(inputs["L0_attn_bu"]).reshape(1, D),
        "fb": f32(inputs["L0_ffn_bias"]).reshape(1, D),
        "Utri": bfc(np.triu(np.ones((128, 128), np.float32))),
        "ident": bfc(np.eye(128, dtype=np.float32)),
        "ones1": bfc(np.ones((128, 128), np.float32)),
        "e3neg": bfc(np.concatenate([-np.eye(3, 128, k=s_, dtype=np.float32)[s_:s_+1] if False else -np.eye(128, dtype=np.float32)[s_:s_+1] for s_ in range(3)], axis=1)),
        "c75": np.full((128, 1), 7.5, np.float32),

    }


def kernel(**inputs):
    x = np.asarray(inputs["L0_attn_hidden_states"], dtype=np.float32)
    B, S, _ = x.shape
    shared = _prep_shared(inputs)
    p = np.arange(128, dtype=np.float32)[:, None]
    in_maps = []
    for c in range(8):
        b, h = c // 2, c % 2
        xpre = np.zeros((NPRE, D), np.float32) if h == 0 else np.ascontiguousarray(x[b, 0:NPRE])
        xown = np.ascontiguousarray(x[b, h * NOWN:(h + 1) * NOWN])
        jl = np.arange(NT_LITE, dtype=np.float32)[None, :]
        jo = np.arange(NT_OWN, dtype=np.float32)[None, :]
        stab = np.concatenate([jl * T + p + 1.0, h * NOWN + jo * T + p + 1.0], axis=1)
        im = dict(shared)
        im["x_all"] = np.concatenate([xpre, xown], axis=0)
        im["stab"] = np.ascontiguousarray(stab.astype(np.float32))
        im["mask"] = np.full((128, 1), float(h), np.float32)
        im["imask"] = np.full((128, 1), 1.0 - float(h), np.float32)
        im["zmg"] = np.zeros((128, 24), np.float32)
        in_maps.append(im)
    nc = _get_nc(_STAGE)
    res = run_bass_kernel_spmd(nc, in_maps, core_ids=list(range(8)))
    out = np.empty((B, S, D), np.float32)
    for c in range(8):
        b, h = c // 2, c % 2
        out[b, h * NOWN:(h + 1) * NOWN] = res.results[c]["o"]
    return out
